# version 19
# baseline (speedup 1.0000x reference)
import math
import contextlib
import numpy as np
import ml_dtypes

import concourse.bass as bass
import concourse.mybir as mybir
from concourse.bass_utils import run_bass_kernel_spmd

F32 = mybir.dt.float32
BF16 = mybir.dt.bfloat16
AF = mybir.ActivationFunctionType
ALU = mybir.AluOpType
AXX = mybir.AxisListType.X

ENGINES = ("tensor", "vector", "scalar", "sync", "gpsimd")
NEG = -1.0e30
MD_OFF = [0, 512, 1024, 1536]


def alibi_slopes(n_heads):
    def pow2_slopes(n):
        start = 2 ** (-(2 ** (-(math.log2(n) - 3))))
        return [start * start**i for i in range(n)]
    if math.log2(n_heads).is_integer():
        slopes = pow2_slopes(n_heads)
    else:
        closest = 2 ** math.floor(math.log2(n_heads))
        slopes = pow2_slopes(closest)
        extra = pow2_slopes(2 * closest)[0::2]
        slopes = slopes + extra[: n_heads - closest]
    return np.asarray(slopes, dtype=np.float32)


def make_consts(T, NH):
    KT = T // 128
    sl = alibi_slopes(NH)
    mdiag = np.zeros((128, 2048), np.float32)
    for pos in range(4):
        j = np.arange(512)[None, :]
        p = np.arange(128)[:, None]
        mdiag[:, MD_OFF[pos]:MD_OFF[pos] + 512] = np.where(j > pos * 128 + p, NEG, 0.0)
    biask = np.zeros((128, NH * KT), np.float32)
    for h in range(NH):
        for kt in range(KT):
            biask[:, h * KT + kt] = -sl[h] * (kt * 128 + np.arange(128))
    slopeq = (sl[:, None] * np.arange(T)[None, :]).astype(np.float32)
    ones2h = np.zeros((128, 33), ml_dtypes.bfloat16)
    ones2h[:64, 0] = 1
    ones2h[64:, 32] = 1
    ones64 = np.zeros((65, 64), np.float32)
    ones64[64, :] = 1.0
    ident = np.eye(128, dtype=ml_dtypes.bfloat16)
    return dict(mdiag=mdiag, biask=biask, slopeq=slopeq, ones2h=ones2h,
                ones64=ones64, ident=ident)


class Sem:
    def __init__(self, handle):
        self.h = handle
        self.n = 0


class B:

    def __init__(self, nc, stack):
        self.nc = nc
        self.stack = stack
        self.q = {e: [] for e in ENGINES}
        self.sems = []

    def sem(self, name):
        s = Sem(self.stack.enter_context(self.nc.semaphore(name)))
        self.sems.append(s)
        return s

    def barrier(self):
        snap = [(s, s.n) for s in self.sems if s.n > 0]
        for eng in ENGINES:
            def f(e, snap=snap):
                for (s, v) in snap:
                    e.wait_ge(s.h, v)
            self.q[eng].append(f)

    def op(self, eng, method, *args, waits=(), inc=None, inck=1, **kwargs):
        wm = {}
        for (s, v) in waits:
            if s is None or v is None or v <= 0:
                continue
            k = id(s)
            if k not in wm or wm[k][1] < v:
                wm[k] = (s, v)
        wl = list(wm.values())

        import traceback
        tb = traceback.extract_stack()[-2]

        def f(e, method=method, args=args, kwargs=kwargs, wl=wl, inc=inc, inck=inck, tb=tb):
            for (s, v) in wl:
                e.wait_ge(s.h, v)
            try:
                ins = getattr(e, method)(*args, **kwargs)
            except Exception:
                print("EMIT FAIL at", tb.filename, tb.lineno, method)
                raise
            if inc is not None:
                ins.then_inc(inc.h, inck)
        self.q[eng].append(f)
        if inc is not None:
            inc.n += inck
            return inc.n
        return None

    def raw(self, eng, fn):
        self.q[eng].append(fn)


def build(T=2048, D=1024, NH=16, E=8, F=4096):
    DT = D // 128
    TTN = T // 128
    KT = T // 128
    QC = T // 512
    NP = NH // 2
    FQ = F // 4
    NFQ = FQ // 128
    FS = 128
    NFS = FS // 128
    SPQ = FQ // FS
    HCN = max(1, D // 512)
    HCW = min(512, D)
    assert NFS >= 1 and SPQ * FS == FQ and QC >= 1

    nc = bass.Bass("TRN2", target_bir_lowering=False)
    x_h = nc.dram_tensor("x", [T, D], F32, kind="ExternalInput").ap()
    wqkv_h = nc.dram_tensor("wqkv", [D, 3 * D], F32, kind="ExternalInput").ap()
    wout_h = nc.dram_tensor("wout", [D, D], F32, kind="ExternalInput").ap()
    wrg_h = nc.dram_tensor("wrg", [128, DT * 2 * E], F32, kind="ExternalInput").ap()
    wg_h = nc.dram_tensor("wg", [E, D, F], F32, kind="ExternalInput").ap()
    wu_h = nc.dram_tensor("wu", [E, D, F], F32, kind="ExternalInput").ap()
    wd_h = nc.dram_tensor("wd", [E, F, D], F32, kind="ExternalInput").ap()
    mdiag_h = nc.dram_tensor("mdiag", [128, 2048], F32, kind="ExternalInput").ap()
    biask_h = nc.dram_tensor("biask", [128, NH * KT], F32, kind="ExternalInput").ap()
    slopeq_h = nc.dram_tensor("slopeq", [NH, T], F32, kind="ExternalInput").ap()
    ones2h_h = nc.dram_tensor("ones2h", [128, 33], BF16, kind="ExternalInput").ap()
    ones64_h = nc.dram_tensor("ones64", [65, 64], F32, kind="ExternalInput").ap()
    ident_h = nc.dram_tensor("ident", [128, 128], BF16, kind="ExternalInput").ap()
    sscr_h = nc.dram_tensor("s_scr", [T, D], F32, kind="Internal").ap()
    out_h = nc.dram_tensor("out", [T, D], F32, kind="ExternalOutput").ap()

    st = contextlib.ExitStack()
    with st:
        b = B(nc, st)
        sb = lambda name, shape, dt: st.enter_context(nc.sbuf_tensor(name, shape, dt))
        xn = sb("xn", [128, DT * T], BF16)
        cw = sb("cw", [128, TTN * E], F32)
        ident = sb("ident_s", [128, 128], BF16)
        mdiag = sb("mdiag_s", [128, 2048], F32)
        biask = sb("biask_s", [128, NH * KT], F32)
        ones2h = sb("ones2h_s", [128, 33], BF16)
        ones64 = sb("ones64_s", [65, 64], F32)
        wrg = sb("wrg_s", [128, DT * 2 * E], F32)
        wrgb = sb("wrgb_s", [128, DT * 2 * E], BF16)

        c_pe = b.sem("c_pe")
        c_ve = b.sem("c_ve")
        c_se = b.sem("c_se")
        s_const = b.sem("s_const")
        s_x = [b.sem(f"s_x{i}") for i in range(3)]
        s_slq = [b.sem(f"s_slq{i}") for i in range(2)]
        s_wp = [b.sem(f"s_wp{i}") for i in range(2)]
        s_wout = b.sem("s_wout")
        s_wg = [b.sem(f"s_wg{i}") for i in range(2)]
        s_wu = [b.sem(f"s_wu{i}") for i in range(2)]
        s_wd = [b.sem(f"s_wd{i}") for i in range(2)]
        s_sw = [b.sem(f"s_sw{i}") for i in range(2)]
        s_si = [b.sem(f"s_si{i}") for i in range(2)]
        s_x2 = [b.sem(f"s_x2{i}") for i in range(2)]
        s_fout = [b.sem(f"s_fout{i}") for i in range(2)]

        for dst, src in ((ident, ident_h), (mdiag, mdiag_h), (biask, biask_h),
                         (ones2h, ones2h_h), (ones64, ones64_h), (wrg, wrg_h)):
            b.op("sync", "dma_start", out=dst[:], in_=src, inc=s_const, inck=16)
        v_const = s_const.n
        v_wrgb = b.op("vector", "tensor_copy", wrgb[:], wrg[:],
                      waits=[(s_const, v_const)], inc=c_ve)

        at = st.enter_context(contextlib.ExitStack())
        asb = lambda name, shape, dt: at.enter_context(nc.sbuf_tensor(name, shape, dt))
        O_sb = asb("O_sb", [128, NP * T], BF16)
        wout_sb = asb("wout_sb", [128, DT * D], BF16)
        qA = [asb(f"qA{i}", [65, T], BF16) for i in range(2)]
        qB = [asb(f"qB{i}", [65, T], BF16) for i in range(2)]
        kA = [asb(f"kA{i}", [65, T], BF16) for i in range(2)]
        kB = [asb(f"kB{i}", [65, T], BF16) for i in range(2)]
        V_sb = [asb(f"V_sb{i}", [128, KT * 130], BF16) for i in range(2)]
        wq_s = [asb(f"wq_s{i}", [128, DT * 128], BF16) for i in range(2)]
        wk_s = [asb(f"wk_s{i}", [128, DT * 128], BF16) for i in range(2)]
        wv_s = [asb(f"wv_s{i}", [128, DT * 128], BF16) for i in range(2)]
        E_b = [asb(f"E_b{i}", [128, T], BF16) for i in range(2)]
        arena = asb("arena", [65, 3 * T], F32)
        mk = asb("mk", [33, 2], F32)
        qsq = asb("qsq", [128, T], BF16)
        bcs = [asb(f"bcs{i}", [64, 512], F32) for i in range(2)]
        s_t = [asb(f"s_t{i}", [128, D], F32) for i in range(2)]
        rsc_lg = asb("rsc_lg", [128, TTN * 2 * E], F32)
        rsc_er = asb("rsc_er", [128, TTN * 2 * E], F32)
        rsc_mx = asb("rsc_mx", [128, 4 * TTN], F32)

        def emit_pair_weights(p, wsem, waits):
            for (dst, c0) in ((wq_s[p % 2], p * 128), (wk_s[p % 2], D + p * 128),
                              (wv_s[p % 2], 2 * D + p * 128)):
                b.op("gpsimd", "dma_start", out=dst.ap().rearrange("p (dt c) -> p dt c", dt=DT),
                     in_=wqkv_h[:, c0:c0 + 128].rearrange("(dt p) c -> p dt c", p=128),
                     waits=waits, inc=wsem, inck=16)
                waits = []
        emit_pair_weights(0, s_wp[0], [])
        emit_pair_weights(1, s_wp[1], [])
        b.op("gpsimd", "dma_start",
             out=wout_sb.ap().rearrange("p (dt c) -> p dt c", dt=DT),
             in_=wout_h.rearrange("(dt p) c -> p dt c", p=128),
             inc=s_wout, inck=16)

        v_xn_evac = [0] * TTN
        with contextlib.ExitStack() as ln:
            lsb = lambda name, shape, dt: ln.enter_context(nc.sbuf_tensor(name, shape, dt))
            x_t = [lsb(f"x_t{i}", [128, D], F32) for i in range(2)]
            xn_tm = [lsb(f"xn_tm{i}", [128, D], BF16) for i in range(2)]
            stats = [lsb(f"stat{i}", [128, 8], F32) for i in range(2)]
            ps_ln = [ln.enter_context(nc.psum_tensor(f"ps_ln{i}", [128, DT * 128], BF16))
                     for i in range(2)]
            x_free = [0, 0]
            stats_free = [0, 0]
            xntm_free = [0, 0]
            psln_free = [0, 0]
            for tt in range(TTN):
                xs, ss = tt % 2, tt % 2
                sx = s_x[xs]
                vx = b.op("sync", "dma_start", out=x_t[xs][:], in_=x_h[tt * 128:(tt + 1) * 128, :],
                          waits=[(c_ve, x_free[xs])], inc=sx, inck=16)
                stt = stats[xs]
                v1 = b.op("scalar", "activation", E_b[0][:, 0:D], x_t[xs][:], AF.Copy,
                          accum_out=stt[:, 0:1],
                          waits=[(sx, vx), (c_ve, stats_free[xs])], inc=c_se)
                v2 = b.op("scalar", "activation", E_b[1][:, 0:D], x_t[xs][:], AF.Square,
                          accum_out=stt[:, 1:2], waits=[(c_se, v1)], inc=c_se)
                v4 = b.op("vector", "tensor_scalar_mul", stt[:, 2:3], stt[:, 0:1], 1.0 / D,
                          waits=[(c_se, v2)], inc=c_ve)
                v5 = b.op("vector", "tensor_scalar_mul", stt[:, 3:4], stt[:, 1:2], 1.0 / D,
                          waits=[(c_ve, v4)], inc=c_ve)
                v6 = b.op("vector", "tensor_mul", stt[:, 4:5], stt[:, 2:3], stt[:, 2:3],
                          waits=[(c_ve, v5)], inc=c_ve)
                v7 = b.op("vector", "tensor_sub", stt[:, 5:6], stt[:, 3:4], stt[:, 4:5],
                          waits=[(c_ve, v6)], inc=c_ve)
                v7b = b.op("vector", "tensor_scalar_add", stt[:, 5:6], stt[:, 5:6], 1e-5,
                           waits=[(c_ve, v7)], inc=c_ve)
                v8 = b.op("scalar", "activation", stt[:, 6:7], stt[:, 5:6], AF.Sqrt,
                          waits=[(c_ve, v7b)], inc=c_se)
                v9 = b.op("vector", "reciprocal", stt[:, 7:8], stt[:, 6:7],
                          waits=[(c_se, v8)], inc=c_ve)
                v10 = b.op("vector", "scalar_tensor_tensor", stt[:, 4:5], stt[:, 2:3], -1.0,
                           stt[:, 7:8], ALU.mult, ALU.mult,
                           waits=[(c_ve, v9)], inc=c_ve)
                v11 = b.op("vector", "tensor_scalar", xn_tm[ss][:], x_t[xs][:],
                           stt[:, 7:8], stt[:, 4:5], ALU.mult, ALU.add,
                           waits=[(c_ve, v10), (c_se, v2), (c_pe, xntm_free[ss])], inc=c_ve)
                x_free[xs] = v11
                stats_free[xs] = v11
                vp = None
                for dt in range(DT):
                    vp = b.op("tensor", "transpose",
                              ps_ln[ss][:, dt * 128:(dt + 1) * 128],
                              xn_tm[ss][:, dt * 128:(dt + 1) * 128], ident[:],
                              waits=([(c_ve, v11), (s_const, v_const),
                                      (c_ve, psln_free[ss])] if dt == 0 else []),
                              inc=(c_pe if dt == DT - 1 else None))
                xntm_free[ss] = vp
                ve = b.op("vector", "tensor_copy",
                          xn.ap().rearrange("p (dt t) -> p dt t", dt=DT)[:, :, tt * 128:(tt + 1) * 128],
                          ps_ln[ss].ap().rearrange("p (dt t) -> p dt t", dt=DT),
                          waits=[(c_pe, vp)], inc=c_ve)
                psln_free[ss] = ve
                v_xn_evac[tt] = ve
        v_xn_all = max(v_xn_evac)
        v_psln_free = [psln_free[0], psln_free[1]]

        pb = [st.enter_context(nc.psum_tensor(f"pb{i}", [128, 512], F32)) for i in range(8)]

        lgall = rsc_lg
        EE = 2 * E
        v_lgc = 0
        bank_free_rt = [0, 0]
        for tt in range(TTN):
            bank = pb[6 + tt % 2]
            vp = None
            for dt in range(DT):
                vp = b.op("tensor", "matmul", bank[:, 0:EE],
                          xn[:, dt * T + tt * 128: dt * T + (tt + 1) * 128],
                          wrgb[:, dt * EE:(dt + 1) * EE],
                          start=(dt == 0), stop=(dt == DT - 1),
                          waits=([(c_ve, max(v_xn_evac[tt], v_wrgb, bank_free_rt[tt % 2]))]
                                 if dt == 0 else []),
                          inc=(c_pe if dt == DT - 1 else None))
            v_lgc = b.op("vector", "tensor_copy", lgall[:, tt * EE:(tt + 1) * EE], bank[:, 0:EE],
                         waits=[(c_pe, vp)], inc=c_ve)
            bank_free_rt[tt % 2] = v_lgc
        lg3 = lgall.ap().rearrange("p (t x) -> p t x", x=EE)
        rv = lg3[:, :, 0:E]
        gv = lg3[:, :, E:EE]
        mxr = rsc_mx[:, 0:TTN]
        mxg = rsc_mx[:, TTN:2 * TTN]
        smr = rsc_mx[:, 2 * TTN:3 * TTN]
        smg = rsc_mx[:, 3 * TTN:4 * TTN]
        era = rsc_er.ap().rearrange("p (t x) -> p t x", x=EE)
        err_ = era[:, :, 0:E]
        erg = era[:, :, E:EE]

        def bc(ap1col, width=E):
            return bass.AP(ap1col.tensor, ap1col.offset,
                           [[ap1col.ap[0][0], 128], [ap1col.ap[1][0], TTN], [0, width]])
        m1 = b.op("vector", "tensor_reduce", mxr.rearrange("p (t e) -> p t e", e=1), rv,
                  AXX, ALU.max, waits=[(c_ve, v_lgc)], inc=c_ve)
        m2 = b.op("vector", "tensor_reduce", mxg.rearrange("p (t e) -> p t e", e=1), gv,
                  AXX, ALU.max, inc=c_ve)
        t1 = b.op("vector", "tensor_tensor", err_, rv, bc(mxr), ALU.subtract,
                  waits=[(c_ve, m2)], inc=c_ve)
        t2 = b.op("vector", "tensor_tensor", erg, gv, bc(mxg), ALU.subtract, inc=c_ve)
        e1 = b.op("scalar", "activation", err_, err_, AF.Exp, waits=[(c_ve, t1)], inc=c_se)
        e2 = b.op("scalar", "activation", erg, erg, AF.Exp, scale=10.0,
                  waits=[(c_ve, t2)], inc=c_se)
        s1r = b.op("vector", "tensor_reduce", smr.rearrange("p (t e) -> p t e", e=1), err_,
                   AXX, ALU.add, waits=[(c_se, e2)], inc=c_ve)
        s2r = b.op("vector", "tensor_reduce", smg.rearrange("p (t e) -> p t e", e=1), erg,
                   AXX, ALU.add, inc=c_ve)
        q1 = b.op("vector", "reciprocal", mxr, smr, waits=[(c_ve, s2r)], inc=c_ve)
        q2 = b.op("vector", "reciprocal", mxg, smg, inc=c_ve)
        n1 = b.op("vector", "tensor_tensor", err_, err_, bc(mxr), ALU.mult,
                  waits=[(c_ve, q2)], inc=c_ve)
        n2 = b.op("vector", "tensor_tensor", erg, erg, bc(mxg), ALU.mult, inc=c_ve)
        cc = b.op("vector", "tensor_tensor", err_, err_, erg, ALU.mult,
                  waits=[(c_ve, n2)], inc=c_ve)
        cs_ = b.op("vector", "tensor_reduce", smr.rearrange("p (t e) -> p t e", e=1), err_,
                   AXX, ALU.add, waits=[(c_ve, cc)], inc=c_ve)
        cs2 = b.op("vector", "tensor_scalar_add", smr, smr, 1e-9, waits=[(c_ve, cs_)], inc=c_ve)
        cr = b.op("vector", "reciprocal", mxr, smr, waits=[(c_ve, cs2)], inc=c_ve)
        cwv = b.op("vector", "tensor_tensor",
                   cw.ap().rearrange("p (t x) -> p t x", x=E), err_, bc(mxr), ALU.mult,
                   waits=[(c_ve, cr)], inc=c_ve)
        v_cw_all = c_ve.n


        vones = 0
        for sl in range(2):
            vones = b.op("vector", "memset", kA[sl][64:65, :], 1.0, inc=c_ve)
            vones = b.op("vector", "memset", kB[sl][64:65, :], 1.0, inc=c_ve)
            for kt in range(KT):
                vones = b.op("vector", "memset", V_sb[sl][:, kt * 130 + 64: kt * 130 + 65], 1.0, inc=c_ve)
                vones = b.op("vector", "memset", V_sb[sl][:, kt * 130 + 129: kt * 130 + 130], 1.0, inc=c_ve)

        slot_pe_free = [0, 0]
        bank_se = {i: 0 for i in range(8)}
        bank_ve = {i: 0 for i in range(8)}
        bank_ve[0] = v_psln_free[0]
        bank_ve[1] = v_psln_free[1]
        bank_ve[6] = bank_free_rt[0]
        bank_ve[7] = bank_free_rt[1]
        v_Ewar = [0, 0]
        sc_rot = 0
        v_rows_last = 0
        v_bcast_pe_c = [0] * QC
        v_recip_prev_c = [0] * QC
        v_oevac_chunk = [0] * QC
        bank_ve_b = [[0], [0]]
        v_qsq_free = [0]
        v_arena_free = [0]
        pending_ochain = [[]]
        st_pair = {}
        emitted_w = {0, 1}

        def flush_ochain():
            while pending_ochain[0]:
                pending_ochain[0].pop(0)()

        def make_qkv_jobs(q):
            sl = q % 2
            wsem = s_wp[sl]
            if q not in emitted_w:
                emit_pair_weights(q, wsem, [(c_pe, slot_pe_free[sl])])
                emitted_w.add(q)
            wtot = wsem.n
            b.op("sync", "dma_start", out=arena[0:1, 2 * T:3 * T],
                 in_=slopeq_h[2 * q:2 * q + 1, :],
                 waits=[(c_ve, v_rows_last)], inc=s_slq[0], inck=16)
            b.op("sync", "dma_start", out=arena[32:33, 2 * T:3 * T],
                 in_=slopeq_h[2 * q + 1:2 * q + 2, :], inc=s_slq[0], inck=16)
            slqtot = s_slq[0].n
            st = dict(v_qe=0, v_ke=0, v_v=0, r6=0, qn_last=0, ksq=0, qsq=0)
            st_pair[q] = st
            jobs = []

            def j_q(c):
                bk = 4 + c % 2
                vp = None
                for dt in range(DT):
                    vp = b.op("tensor", "matmul", pb[bk][:, :],
                              wq_s[sl][:, dt * 128:(dt + 1) * 128],
                              xn[:, dt * T + c * 512: dt * T + (c + 1) * 512],
                              start=(dt == 0), stop=(dt == DT - 1),
                              waits=([(wsem, wtot), (c_pe, slot_pe_free[sl]),
                                      (c_se, bank_se[bk]),
                                      (c_ve, max(bank_ve[bk], v_xn_evac[min(4 * c + 3, TTN - 1)]))]
                                     if dt == 0 else []),
                              inc=(c_pe if dt == DT - 1 else None))
                b.op("scalar", "activation", qA[sl][0:64, c * 512:(c + 1) * 512], pb[bk][0:64, :],
                     AF.Copy, scale=0.125, waits=[(c_pe, vp)], inc=c_se)
                st["v_qe"] = b.op("scalar", "activation", qB[sl][0:64, c * 512:(c + 1) * 512],
                                  pb[bk][64:128, :], AF.Copy, scale=0.125, inc=c_se)
                bank_se[bk] = st["v_qe"]

            def j_k(c):
                bk = 4 + c % 2
                vp = None
                for dt in range(DT):
                    vp = b.op("tensor", "matmul", pb[bk][:, :],
                              wk_s[sl][:, dt * 128:(dt + 1) * 128],
                              xn[:, dt * T + c * 512: dt * T + (c + 1) * 512],
                              start=(dt == 0), stop=(dt == DT - 1),
                              waits=([(c_se, bank_se[bk]),
                                      (c_ve, max(bank_ve[bk], v_xn_evac[min(4 * c + 3, TTN - 1)]))]
                                     if dt == 0 else []),
                              inc=(c_pe if dt == DT - 1 else None))
                b.op("vector", "tensor_copy", kA[sl][0:64, c * 512:(c + 1) * 512], pb[bk][0:64, :],
                     waits=[(c_pe, vp)], inc=c_ve)
                st["v_ke"] = b.op("vector", "tensor_copy", kB[sl][0:64, c * 512:(c + 1) * 512],
                                  pb[bk][64:128, :], inc=c_ve)
                bank_ve[bk] = st["v_ke"]

            def j_v(tt):
                bk = 4 + tt % 2
                vp = None
                for dt in range(DT):
                    vp = b.op("tensor", "matmul", pb[bk][:, 0:128],
                              xn[:, dt * T + tt * 128: dt * T + (tt + 1) * 128],
                              wv_s[sl][:, dt * 128:(dt + 1) * 128],
                              start=(dt == 0), stop=(dt == DT - 1),
                              waits=([(c_se, bank_se[bk]),
                                      (c_ve, max(bank_ve[bk], v_xn_evac[tt]))] if dt == 0 else []),
                              inc=(c_pe if dt == DT - 1 else None))
                b.op("vector", "tensor_copy", V_sb[sl][:, tt * 130: tt * 130 + 64], pb[bk][:, 0:64],
                     waits=[(c_pe, vp)], inc=c_ve)
                st["v_v"] = b.op("vector", "tensor_copy",
                                 V_sb[sl][:, tt * 130 + 65: tt * 130 + 129],
                                 pb[bk][:, 64:128], inc=c_ve)
                bank_ve[bk] = st["v_v"]

            def j_qsq():
                b.op("scalar", "activation", qsq[0:64, :], qA[sl][0:64, :], AF.Square,
                     waits=[(c_se, st["v_qe"]), (c_pe, v_qsq_free[0])], inc=c_se)
                st["qsq"] = b.op("scalar", "activation", qsq[64:128, :], qB[sl][0:64, :],
                                 AF.Square, inc=c_se)

            def j_ksq():
                b.op("scalar", "activation", qsq[0:64, :], kA[sl][0:64, :], AF.Square,
                     waits=[(c_ve, st["v_ke"]), (c_pe, st["qn_last"])], inc=c_se)
                st["ksq"] = b.op("scalar", "activation", qsq[64:128, :], kB[sl][0:64, :],
                                 AF.Square, inc=c_se)

            def j_norm(kind, c):
                bk = 4 + c % 2
                vsq = st["qsq"] if kind == "q" else st["ksq"]
                vp = b.op("tensor", "matmul", pb[bk][0:33, :], ones2h[:, 0:33],
                          qsq[:, c * 512:(c + 1) * 512], start=True, stop=True,
                          waits=[(s_const, v_const), (c_se, max(vsq, bank_se[bk])),
                                 (c_ve, max(bank_ve[bk], v_arena_free[0]))], inc=c_pe)
                if kind == "q":
                    st["qn_last"] = vp
                v_qsq_free[0] = vp
                dst = arena[0:33, c * 512:(c + 1) * 512] if kind == "q" else \
                    arena[0:33, T + c * 512: T + (c + 1) * 512]
                bank_ve[bk] = b.op("vector", "tensor_copy", dst, pb[bk][0:33, :],
                                   waits=[(c_pe, vp)], inc=c_ve)
                st["ne"] = bank_ve[bk]

            def j_rows():
                r1 = b.op("vector", "tensor_reduce", mk[:, 0:1], arena[0:33, T:2 * T], AXX,
                          ALU.max, waits=[(c_ve, st["ne"])], inc=c_ve)
                r2 = b.op("scalar", "activation", mk[:, 1:2], mk[:, 0:1], AF.Sqrt,
                          waits=[(c_ve, r1)], inc=c_se)
                r3 = b.op("scalar", "activation", arena[0:33, T:2 * T], arena[0:33, 0:T], AF.Sqrt,
                          waits=[(c_ve, r1)], inc=c_se)
                r4 = b.op("vector", "tensor_scalar_mul", arena[0:33, 0:T], arena[0:33, T:2 * T],
                          mk[:, 1:2], waits=[(c_se, max(r2, r3))], inc=c_ve)
                r5 = b.op("vector", "tensor_sub", qA[sl][64:65, :], arena[0:1, 2 * T:3 * T],
                          arena[0:1, 0:T],
                          waits=[(c_ve, r4), (s_slq[0], slqtot),
                                 (c_pe, slot_pe_free[sl])], inc=c_ve)
                r6 = b.op("vector", "tensor_sub", qB[sl][64:65, :], arena[32:33, 2 * T:3 * T],
                          arena[32:33, 0:T], inc=c_ve)
                st["r6"] = r6
                v_arena_free[0] = r6
            for c in range(QC):
                jobs.append(lambda c=c: j_q(c))
            for c in range(QC):
                jobs.append(lambda c=c: j_k(c))
            jobs.append(j_qsq)
            nq = list(range(QC))
            nk = list(range(QC))
            vts = list(range(TTN))
            seqj = []
            while vts or nq or nk:
                if vts:
                    seqj.append(("v", vts.pop(0)))
                if nq:
                    seqj.append(("nq", nq.pop(0)))
                elif nk is not None and len(nk) == QC and not nq:
                    seqj.append(("ksq", None))
                    seqj.append(("nk", nk.pop(0)))
                elif nk:
                    seqj.append(("nk", nk.pop(0)))
            for kind, cc in seqj:
                if kind == "v":
                    jobs.append(lambda tt=cc: j_v(tt))
                elif kind == "nq":
                    jobs.append(lambda c=cc: j_norm("q", c))
                elif kind == "ksq":
                    jobs.append(j_ksq)
                else:
                    jobs.append(lambda c=cc: j_norm("k", c))
            def j_rows_wrap():
                j_rows()
            jobs.append(j_rows_wrap)
            return jobs

        def finish_rows(q):
            pass

        jobs_next = make_qkv_jobs(0)
        while jobs_next:
            jobs_next.pop(0)()
        v_rows_last = st_pair[0]["r6"]

        for p in range(NP):
            sl = p % 2
            if p + 1 < NP:
                jobs_next = make_qkv_jobs(p + 1)
            else:
                jobs_next = []
            stp = st_pair[p]
            r6 = stp["r6"]
            for h in range(2):
                qX = qA[sl] if h == 0 else qB[sl]
                kX = kA[sl] if h == 0 else kB[sl]
                hg = 2 * p + h
                orow0 = 64 * h
                vex_kt = [0] * KT
                v_pv_last = 0
                for kt in range(KT + 1):
                    if kt < KT:
                        cs = kt // 4
                        ebk = kt % 2
                        for c in range(0, cs + 1):
                            sbk = 6 + (sc_rot % 2)
                            sc_rot += 1
                            w_mm = [(c_se, bank_se[sbk]), (c_ve, bank_ve[sbk])]
                            if kt == 0 and c == 0:
                                w_mm += [(c_ve, max(stp["v_v"], r6)), (c_se, stp["v_qe"])]
                            vsc = b.op("tensor", "matmul", pb[sbk][:, :],
                                       kX[:, kt * 128:(kt + 1) * 128],
                                       qX[:, c * 512:(c + 1) * 512], start=True, stop=True,
                                       waits=w_mm, inc=c_pe)
                            ew = [(c_pe, vsc), (c_pe, v_Ewar[ebk])]
                            if c == cs:
                                pos = kt % 4
                                w0 = pos * 128
                                vstr = b.op("vector", "tensor_tensor", pb[sbk][:, w0:512],
                                            pb[sbk][:, w0:512],
                                            mdiag[:, MD_OFF[pos] + w0:MD_OFF[pos] + 512],
                                            ALU.add,
                                            waits=[(c_pe, vsc), (s_const, v_const),
                                                   (c_se, bank_se[sbk])], inc=c_ve)
                                bank_ve[sbk] = vstr
                                ew.append((c_ve, vstr))
                            vex = b.op("scalar", "activation",
                                       E_b[ebk][:, c * 512:(c + 1) * 512], pb[sbk][:, :],
                                       AF.Exp, bias=biask[:, hg * KT + kt: hg * KT + kt + 1],
                                       waits=ew + [(s_const, v_const)], inc=c_se)
                            bank_se[sbk] = vex
                        vex_kt[kt] = vex
                    if kt == 1:
                        flush_ochain()
                    if kt >= 1:
                        k2 = kt - 1
                        ebk2 = k2 % 2
                        cs2 = k2 // 4
                        vpv = None
                        for c in range(cs2 + 1):
                            w_pv = [(c_se, vex_kt[k2])] if c == 0 else []
                            if k2 == 4 * c:
                                w_pv.append((c_ve, bank_ve[c]))
                            vpv = b.op("tensor", "matmul", pb[c][0:65, :],
                                       V_sb[sl][:, k2 * 130 + h * 65: k2 * 130 + h * 65 + 65],
                                       E_b[ebk2][:, c * 512:(c + 1) * 512],
                                       start=(k2 == 4 * c), stop=(k2 == KT - 1),
                                       waits=w_pv,
                                       inc=(c_pe if c == cs2 else None))
                        v_Ewar[ebk2] = vpv
                        if k2 == KT - 1:
                            v_pv_last = vpv
                    if kt >= 2:
                        for _ in range(3):
                            if jobs_next:
                                jobs_next.pop(0)()
                v_recip_c = [0] * QC
                for c in range(QC):
                    vsum = b.op("scalar", "activation",
                                arena[64:65, c * 512:(c + 1) * 512], pb[c][64:65, :], AF.Copy,
                                waits=[(c_pe, max(v_pv_last, v_bcast_pe_c[c])),
                                       (c_se, v_recip_prev_c[c])], inc=c_se)
                    vln = b.op("scalar", "activation",
                               arena[64:65, T + c * 512: T + (c + 1) * 512],
                               arena[64:65, c * 512:(c + 1) * 512], AF.Ln,
                               waits=[(c_se, vsum), (c_pe, v_bcast_pe_c[c])], inc=c_se)
                    v_recip_c[c] = b.op("scalar", "activation",
                                        arena[64:65, T + c * 512: T + (c + 1) * 512],
                                        arena[64:65, T + c * 512: T + (c + 1) * 512],
                                        AF.Exp, scale=-1.0,
                                        waits=[(c_se, vln)], inc=c_se)
                    v_recip_prev_c[c] = v_recip_c[c]

                def make_ochain(p=p, orow0=orow0, vrc=tuple(v_recip_c)):
                    def emit():
                        vbc = {}
                        vbe = {}
                        vtt = {}

                        def em_bc(c):
                            bk = 4 + c % 2
                            w_ = [(c_ve, bank_ve[bk]), (c_se, vrc[c]), (s_const, v_const)]
                            if c >= 2:
                                w_.append((c_se, vbe[c - 2]))
                            vbc[c] = b.op("tensor", "matmul", pb[bk][0:64, :],
                                          ones64[64:65, 0:64],
                                          arena[64:65, T + c * 512: T + (c + 1) * 512],
                                          start=True, stop=True, waits=w_, inc=c_pe)
                            v_bcast_pe_c[c] = vbc[c]

                        def em_ev(c):
                            bk = 4 + c % 2
                            w_ = [(c_pe, vbc[c]), (c_se, bank_se[bk])]
                            if c >= 2:
                                w_.append((c_ve, vtt[c - 2]))
                            vbe[c] = b.op("scalar", "activation", bcs[c % 2][:, :],
                                          pb[bk][0:64, :], AF.Copy, waits=w_, inc=c_se)
                            bank_se[bk] = vbe[c]

                        def em_tt(c):
                            vtt[c] = b.op("vector", "tensor_tensor",
                                          O_sb[orow0:orow0 + 64,
                                               p * T + c * 512: p * T + (c + 1) * 512],
                                          pb[c][0:64, :], bcs[c % 2][:, :], ALU.mult,
                                          waits=[(c_se, vbe[c]), (c_ve, bank_ve[c])], inc=c_ve)
                            bank_ve[c] = vtt[c]
                            v_oevac_chunk[c] = vtt[c]
                        final = []
                        ev_done = []
                        tt_done = []
                        bi = 0
                        while len(tt_done) < QC:
                            if bi < QC and (bi < 2 or (bi - 2) < len(ev_done)):
                                final.append(("bc", bi))
                                bi += 1
                                continue
                            ce = len(ev_done)
                            if ce < QC and ce < bi and (ce < 2 or (ce - 2) < len(tt_done)):
                                final.append(("ev", ce))
                                ev_done.append(ce)
                                continue
                            ct = len(tt_done)
                            final.append(("tt", ct))
                            tt_done.append(ct)
                        for kind, c in final:
                            if kind == "bc":
                                em_bc(c)
                            elif kind == "ev":
                                em_ev(c)
                            else:
                                em_tt(c)
                    return emit
                pending_ochain[0] = [make_ochain()]
            while jobs_next:
                jobs_next.pop(0)()
            if p + 1 < NP:
                v_rows_last = st_pair[p + 1]["r6"]
            slot_pe_free[sl] = c_pe.n
        flush_ochain()
        v_oevac = max(v_oevac_chunk)

        sbank_free = {}
        st_free = [0, 0]
        for qt in range(TTN):
            slo = qt % 2
            vlast = 0
            for c in range(HCN):
                bank = pb[c + 2 * (qt % 2)]
                vp = None
                for pp in range(DT):
                    vp = b.op("tensor", "matmul", bank[:, 0:HCW],
                              O_sb[:, pp * T + qt * 128: pp * T + (qt + 1) * 128],
                              wout_sb[:, pp * D + c * HCW: pp * D + (c + 1) * HCW],
                              start=(pp == 0), stop=(pp == DT - 1),
                              waits=([(c_ve, max(v_oevac, sbank_free.get(id(bank), 0))),
                                      (s_wout, 16)] if pp == 0 else []),
                              inc=(c_pe if pp == DT - 1 else None))
                vlast = b.op("vector", "tensor_copy", s_t[slo][:, c * HCW:(c + 1) * HCW],
                             bank[:, 0:HCW],
                             waits=[(c_pe, vp), (s_sw[slo], st_free[slo])], inc=c_ve)
                sbank_free[id(bank)] = vlast
            st_free[slo] = b.op("sync", "dma_start", out=sscr_h[qt * 128:(qt + 1) * 128, :],
                                in_=s_t[slo][:], waits=[(c_ve, vlast)], inc=s_sw[slo], inck=16)

        at.close()
        v_sproj_ve = c_ve.n
        b.barrier()

        moe_st = contextlib.ExitStack()
        with moe_st:
            moe = moe_st.enter_context(nc.sbuf_tensor("moe", [128, TTN * D], F32))
            x2 = [moe_st.enter_context(nc.sbuf_tensor(f"x2_{i}", [128, D], F32)) for i in range(2)]
            s_in = [moe_st.enter_context(nc.sbuf_tensor(f"s_in{i}", [128, D], F32)) for i in range(2)]
            moe_last = [[0, 0] for _ in range(TTN)]
            first_acc_tt = [True] * TTN
            with contextlib.ExitStack() as f1:
                gsb = lambda name, shape, dt: f1.enter_context(nc.sbuf_tensor(name, shape, dt))
                h_sb = gsb("h_sb", [128, NFQ * T], BF16)
                wgs = [gsb(f"wgs{i}", [128, DT * FS], BF16) for i in range(2)]
                wus = [gsb(f"wus{i}", [128, DT * FS], BF16) for i in range(2)]
                wds = [gsb(f"wds{i}", [128, NFQ * D], BF16) for i in range(2)]
                sg = gsb("sg", [128, 2 * T], BF16)

                wgs_free = [0, 0]
                wus_free = [0, 0]
                wds_free = [0, 0]
                v_sg_war = [0, 0]
                v_silu_last = 0
                v_htt_last = v_sproj_ve
                fbank_ve = [v_sproj_ve] * 8
                v_h_ft = [0] * NFQ
                v_down_done = 0
                nslice = 0
                for e in range(E):
                    for qt in range(4):
                        for fe in range(SPQ):
                            wsl = nslice % 2
                            nslice += 1
                            f0 = qt * FQ + fe * FS
                            b.op("gpsimd", "dma_start",
                                 out=wgs[wsl].ap().rearrange("p (dt f) -> p dt f", dt=DT),
                                 in_=wg_h[e, :, f0:f0 + FS].rearrange("(dt p) f -> p dt f", p=128),
                                 waits=[(c_pe, wgs_free[wsl])], inc=s_wg[wsl], inck=16)
                            b.op("gpsimd", "dma_start",
                                 out=wus[wsl].ap().rearrange("p (dt f) -> p dt f", dt=DT),
                                 in_=wu_h[e, :, f0:f0 + FS].rearrange("(dt p) f -> p dt f", p=128),
                                 waits=[(c_pe, wus_free[wsl])], inc=s_wu[wsl], inck=16)
                            for ftl in range(NFS):
                                ftq = fe * NFS + ftl
                                vg = None
                                for dt in range(DT):
                                    for c in range(QC):
                                        vg = b.op("tensor", "matmul", pb[c][:, :],
                                                  wgs[wsl][:, dt * FS + ftl * 128: dt * FS + (ftl + 1) * 128],
                                                  xn[:, dt * T + c * 512: dt * T + (c + 1) * 512],
                                                  start=(dt == 0), stop=(dt == DT - 1),
                                                  waits=([(s_wg[wsl], s_wg[wsl].n),
                                                          (c_se, v_silu_last),
                                                          (c_ve, max(max(fbank_ve[0:4]), v_xn_all))]
                                                         if (dt == 0 and c == 0) else []),
                                                  inc=(c_pe if (dt == DT - 1 and c == QC - 1) else None))
                                if ftl == NFS - 1:
                                    wgs_free[wsl] = vg
                                sgh = (ftq % 2) * T
                                vsl = 0
                                for c in range(QC):
                                    vsl = b.op("scalar", "activation",
                                               sg[:, sgh + c * 512: sgh + (c + 1) * 512],
                                               pb[c][:, :], AF.Silu,
                                               waits=([(c_pe, vg), (c_ve, v_sg_war[ftq % 2])]
                                                      if c == 0 else []), inc=c_se)
                                v_silu_last = vsl
                                vu = None
                                for dt in range(DT):
                                    for c in range(QC):
                                        vu = b.op("tensor", "matmul", pb[4 + c][:, :],
                                                  wus[wsl][:, dt * FS + ftl * 128: dt * FS + (ftl + 1) * 128],
                                                  xn[:, dt * T + c * 512: dt * T + (c + 1) * 512],
                                                  start=(dt == 0), stop=(dt == DT - 1),
                                                  waits=([(s_wu[wsl], s_wu[wsl].n),
                                                          (c_ve, max(v_htt_last, max(fbank_ve[4:8])))]
                                                         if (dt == 0 and c == 0) else []),
                                                  inc=(c_pe if (dt == DT - 1 and c == QC - 1) else None))
                                if ftl == NFS - 1:
                                    wus_free[wsl] = vu
                                vh = 0
                                for c in range(QC):
                                    vh = b.op("vector", "tensor_tensor",
                                              h_sb[:, ftq * T + c * 512: ftq * T + (c + 1) * 512],
                                              pb[4 + c][:, :], sg[:, sgh + c * 512: sgh + (c + 1) * 512],
                                              ALU.mult,
                                              waits=([(c_pe, max(vu, v_down_done)), (c_se, vsl)]
                                                     if c == 0 else []), inc=c_ve)
                                v_htt_last = vh
                                v_sg_war[ftq % 2] = vh
                                v_h_ft[ftq] = vh
                        wdsl = (e * 4 + qt) % 2
                        b.op("gpsimd", "dma_start",
                             out=wds[wdsl].ap().rearrange("p (ft d) -> p ft d", ft=NFQ),
                             in_=wd_h[e, qt * FQ:(qt + 1) * FQ, :].rearrange("(ft p) d -> p ft d", p=128),
                             waits=[(c_pe, wds_free[wdsl])], inc=s_wd[wdsl], inck=16)
                        vdn = None
                        for tg in range(TTN // 2):
                            bset = (tg % 2) * 4
                            for ftq in range(NFQ):
                                for ttl in range(2):
                                    tt = tg * 2 + ttl
                                    for hc in range(HCN):
                                        bank = pb[bset + ttl * 2 + hc]
                                        w_ = []
                                        if ttl == 0 and hc == 0 and tg <= 1:
                                            w_ = [(c_ve, v_h_ft[ftq])]
                                        if ftq == 0 and ttl == 0 and hc == 0:
                                            w_ += [(s_wd[wdsl], s_wd[wdsl].n),
                                                   (c_se, v_silu_last),
                                                   (c_ve, max(fbank_ve[bset:bset + 4]))]
                                        vdn = b.op("tensor", "matmul", bank[:, 0:HCW],
                                                   h_sb[:, ftq * T + tt * 128: ftq * T + (tt + 1) * 128],
                                                   wds[wdsl][:, ftq * D + hc * HCW: ftq * D + (hc + 1) * HCW],
                                                   start=(ftq == 0), stop=(ftq == NFQ - 1),
                                                   waits=w_,
                                                   inc=(c_pe if (ftq == NFQ - 1 and ttl == 1 and hc == HCN - 1) else None))
                            for ttl in range(2):
                                tt = tg * 2 + ttl
                                for hc in range(HCN):
                                    bank = pb[bset + ttl * 2 + hc]
                                    dst = moe[:, tt * D + hc * HCW: tt * D + (hc + 1) * HCW]
                                    cwc = cw[:, tt * E + e: tt * E + e + 1]
                                    if first_acc_tt[tt]:
                                        vm = b.op("vector", "tensor_scalar_mul", dst, bank[:, 0:HCW], cwc,
                                                  waits=[(c_pe, vdn), (c_ve, v_cw_all)], inc=c_ve)
                                    else:
                                        vm = b.op("vector", "scalar_tensor_tensor", dst, bank[:, 0:HCW],
                                                  cwc, dst, ALU.mult, ALU.add,
                                                  waits=[(c_pe, vdn), (c_ve, moe_last[tt][hc])], inc=c_ve)
                                    moe_last[tt][hc] = vm
                                    fbank_ve[bset + ttl * 2 + hc] = vm
                                first_acc_tt[tt] = False
                        v_down_done = c_pe.n

            x2_free = [0, 0]
            si_free = [0, 0]
            o_free = [0, 0]
            for tt in range(TTN):
                slo = tt % 2
                vx = b.op("sync", "dma_start", out=x2[slo][:], in_=x_h[tt * 128:(tt + 1) * 128, :],
                          waits=[(c_ve, x2_free[slo])], inc=s_x2[slo], inck=16)
                vsi = b.op("sync", "dma_start", out=s_in[slo][:],
                           in_=sscr_h[tt * 128:(tt + 1) * 128, :],
                           waits=[(c_ve, si_free[slo]), (s_fout[slo], o_free[slo]),
                                  (s_sw[0], s_sw[0].n), (s_sw[1], s_sw[1].n)],
                           inc=s_si[slo], inck=16)
                mslice = moe[:, tt * D:(tt + 1) * D]
                vg = b.op("scalar", "activation", mslice, mslice, AF.Silu,
                          waits=[(c_ve, max(moe_last[tt][0], moe_last[tt][1]))], inc=c_se)
                vc = b.op("vector", "tensor_mul", mslice, s_in[slo][:], mslice,
                          waits=[(c_se, vg), (s_si[slo], vsi)], inc=c_ve)
                vo = b.op("vector", "tensor_add", s_in[slo][:], mslice, x2[slo][:],
                          waits=[(c_ve, vc), (s_x2[slo], vx)], inc=c_ve)
                x2_free[slo] = vo
                si_free[slo] = vo
                o_free[slo] = b.op("sync", "dma_start", out=out_h[tt * 128:(tt + 1) * 128, :],
                                   in_=s_in[slo][:], waits=[(c_ve, vo)], inc=s_fout[slo], inck=16)
            b.raw("sync", lambda e, s0=s_fout[0], v0=s_fout[0].n, s1=s_fout[1], v1=s_fout[1].n:
                  (e.wait_ge(s0.h, v0), e.wait_ge(s1.h, v1)))

        blk = st.enter_context(nc.Block())
        for eng in ENGINES:
            fns = b.q[eng]
            def runner(e, fns=fns):
                for f in fns:
                    f(e)
            getattr(blk, eng)(runner)
    return nc


def prep_inputs(L0_attn_x, L0_attn_W_qkv, L0_attn_W_out, L0_ffn_w_router, L0_ffn_w_gating,
                L0_ffn_w_gate, L0_ffn_w_up, L0_ffn_w_down, T, D, NH):
    DT = D // 128
    E = np.asarray(L0_ffn_w_router).shape[1]
    x = np.ascontiguousarray(np.asarray(L0_attn_x, dtype=np.float32))
    wqkv = np.ascontiguousarray(np.asarray(L0_attn_W_qkv, dtype=np.float32))
    wout = np.ascontiguousarray(np.asarray(L0_attn_W_out, dtype=np.float32))
    wr = np.asarray(L0_ffn_w_router, dtype=np.float32).reshape(DT, 128, E)
    wgt = np.asarray(L0_ffn_w_gating, dtype=np.float32).reshape(DT, 128, E)
    wrg = np.ascontiguousarray(
        np.concatenate([wr, wgt], axis=2).transpose(1, 0, 2).reshape(128, DT * 2 * E))
    wg = np.ascontiguousarray(np.asarray(L0_ffn_w_gate, dtype=np.float32))
    wu = np.ascontiguousarray(np.asarray(L0_ffn_w_up, dtype=np.float32))
    wd = np.ascontiguousarray(np.asarray(L0_ffn_w_down, dtype=np.float32))
    cst = make_consts(T, NH)
    base = dict(wqkv=wqkv, wout=wout, wrg=wrg, wg=wg, wu=wu, wd=wd,
                mdiag=cst["mdiag"], biask=cst["biask"], slopeq=cst["slopeq"],
                ones2h=cst["ones2h"], ones64=cst["ones64"], ident=cst["ident"])
    return x, base


def kernel(L0_attn_x, L0_attn_W_qkv, L0_attn_W_out, L0_ffn_w_router, L0_ffn_w_gating,
           L0_ffn_w_gate, L0_ffn_w_up, L0_ffn_w_down):
    T, D, NH, E, F = 2048, 1024, 16, 8, 4096
    x, base = prep_inputs(L0_attn_x, L0_attn_W_qkv, L0_attn_W_out, L0_ffn_w_router,
                          L0_ffn_w_gating, L0_ffn_w_gate, L0_ffn_w_up, L0_ffn_w_down, T, D, NH)
    nc = build(T, D, NH, E, F)
    in_maps = [dict(base, x=np.ascontiguousarray(x[bb])) for bb in range(x.shape[0])]
    res = run_bass_kernel_spmd(nc, in_maps, core_ids=list(range(8)))
    out = np.stack([np.asarray(r["out"]) for r in res.results], axis=0).astype(np.float32)
    return out


# version 20 (final)
# speedup vs baseline: 1.0494x; 1.0494x over previous
import math
import contextlib
import numpy as np
import ml_dtypes

import concourse.bass as bass
import concourse.mybir as mybir
from concourse.bass_utils import run_bass_kernel_spmd

F32 = mybir.dt.float32
BF16 = mybir.dt.bfloat16
AF = mybir.ActivationFunctionType
ALU = mybir.AluOpType
AXX = mybir.AxisListType.X

ENGINES = ("tensor", "vector", "scalar", "sync", "gpsimd")
NEG = -1.0e30
MD_OFF = [0, 512, 1024, 1536]


def alibi_slopes(n_heads):
    def pow2_slopes(n):
        start = 2 ** (-(2 ** (-(math.log2(n) - 3))))
        return [start * start**i for i in range(n)]
    if math.log2(n_heads).is_integer():
        slopes = pow2_slopes(n_heads)
    else:
        closest = 2 ** math.floor(math.log2(n_heads))
        slopes = pow2_slopes(closest)
        extra = pow2_slopes(2 * closest)[0::2]
        slopes = slopes + extra[: n_heads - closest]
    return np.asarray(slopes, dtype=np.float32)


def make_consts(T, NH):
    KT = T // 128
    sl = alibi_slopes(NH)
    mdiag = np.zeros((128, 2048), np.float32)
    for pos in range(4):
        j = np.arange(512)[None, :]
        p = np.arange(128)[:, None]
        mdiag[:, MD_OFF[pos]:MD_OFF[pos] + 512] = np.where(j > pos * 128 + p, NEG, 0.0)
    biask = np.zeros((128, NH * KT), np.float32)
    for h in range(NH):
        for kt in range(KT):
            biask[:, h * KT + kt] = -sl[h] * (kt * 128 + np.arange(128))
    slopeq = (sl[:, None] * np.arange(T)[None, :]).astype(np.float32)
    ones2h = np.zeros((128, 33), ml_dtypes.bfloat16)
    ones2h[:64, 0] = 1
    ones2h[64:, 32] = 1
    ones64 = np.zeros((65, 64), np.float32)
    ones64[64, :] = 1.0
    ident = np.eye(128, dtype=ml_dtypes.bfloat16)
    return dict(mdiag=mdiag, biask=biask, slopeq=slopeq, ones2h=ones2h,
                ones64=ones64, ident=ident)


class Sem:
    def __init__(self, handle):
        self.h = handle
        self.n = 0


class B:

    def __init__(self, nc, stack):
        self.nc = nc
        self.stack = stack
        self.q = {e: [] for e in ENGINES}
        self.sems = []

    def sem(self, name):
        s = Sem(self.stack.enter_context(self.nc.semaphore(name)))
        self.sems.append(s)
        return s

    def barrier(self):
        snap = [(s, s.n) for s in self.sems if s.n > 0]
        for eng in ENGINES:
            def f(e, snap=snap):
                for (s, v) in snap:
                    e.wait_ge(s.h, v)
            self.q[eng].append(f)

    def op(self, eng, method, *args, waits=(), inc=None, inck=1, **kwargs):
        wm = {}
        for (s, v) in waits:
            if s is None or v is None or v <= 0:
                continue
            k = id(s)
            if k not in wm or wm[k][1] < v:
                wm[k] = (s, v)
        wl = list(wm.values())

        import traceback
        tb = traceback.extract_stack()[-2]

        def f(e, method=method, args=args, kwargs=kwargs, wl=wl, inc=inc, inck=inck, tb=tb):
            for (s, v) in wl:
                e.wait_ge(s.h, v)
            try:
                ins = getattr(e, method)(*args, **kwargs)
            except Exception:
                print("EMIT FAIL at", tb.filename, tb.lineno, method)
                raise
            if inc is not None:
                ins.then_inc(inc.h, inck)
        self.q[eng].append(f)
        if inc is not None:
            inc.n += inck
            return inc.n
        return None

    def raw(self, eng, fn):
        self.q[eng].append(fn)


def build(T=2048, D=1024, NH=16, E=8, F=4096):
    DT = D // 128
    TTN = T // 128
    KT = T // 128
    QC = T // 512
    NP = NH // 2
    FQ = F // 4
    NFQ = FQ // 128
    FS = 128
    NFS = FS // 128
    SPQ = FQ // FS
    HCN = max(1, D // 512)
    HCW = min(512, D)
    assert NFS >= 1 and SPQ * FS == FQ and QC >= 1

    nc = bass.Bass("TRN2", target_bir_lowering=False)
    x_h = nc.dram_tensor("x", [T, D], F32, kind="ExternalInput").ap()
    wqkv_h = nc.dram_tensor("wqkv", [D, 3 * D], F32, kind="ExternalInput").ap()
    wout_h = nc.dram_tensor("wout", [D, D], F32, kind="ExternalInput").ap()
    wrg_h = nc.dram_tensor("wrg", [128, DT * 2 * E], F32, kind="ExternalInput").ap()
    wg_h = nc.dram_tensor("wg", [E, D, F], F32, kind="ExternalInput").ap()
    wu_h = nc.dram_tensor("wu", [E, D, F], F32, kind="ExternalInput").ap()
    wd_h = nc.dram_tensor("wd", [E, F, D], F32, kind="ExternalInput").ap()
    mdiag_h = nc.dram_tensor("mdiag", [128, 2048], F32, kind="ExternalInput").ap()
    biask_h = nc.dram_tensor("biask", [128, NH * KT], F32, kind="ExternalInput").ap()
    slopeq_h = nc.dram_tensor("slopeq", [NH, T], F32, kind="ExternalInput").ap()
    ones2h_h = nc.dram_tensor("ones2h", [128, 33], BF16, kind="ExternalInput").ap()
    ones64_h = nc.dram_tensor("ones64", [65, 64], F32, kind="ExternalInput").ap()
    ident_h = nc.dram_tensor("ident", [128, 128], BF16, kind="ExternalInput").ap()
    sscr_h = nc.dram_tensor("s_scr", [T, D], F32, kind="Internal").ap()
    out_h = nc.dram_tensor("out", [T, D], F32, kind="ExternalOutput").ap()

    st = contextlib.ExitStack()
    with st:
        b = B(nc, st)
        sb = lambda name, shape, dt: st.enter_context(nc.sbuf_tensor(name, shape, dt))
        xn = sb("xn", [128, DT * T], BF16)
        cw = sb("cw", [128, TTN * E], F32)
        ident = sb("ident_s", [128, 128], BF16)
        mdiag = sb("mdiag_s", [128, 2048], F32)
        biask = sb("biask_s", [128, NH * KT], F32)
        ones2h = sb("ones2h_s", [128, 33], BF16)
        ones64 = sb("ones64_s", [65, 64], F32)
        wrg = sb("wrg_s", [128, DT * 2 * E], F32)
        wrgb = sb("wrgb_s", [128, DT * 2 * E], BF16)

        c_pe = b.sem("c_pe")
        c_ve = b.sem("c_ve")
        c_se = b.sem("c_se")
        s_const = b.sem("s_const")
        s_x = [b.sem(f"s_x{i}") for i in range(3)]
        s_slq = [b.sem(f"s_slq{i}") for i in range(2)]
        s_wp = [b.sem(f"s_wp{i}") for i in range(2)]
        s_wout = b.sem("s_wout")
        s_wg = [b.sem(f"s_wg{i}") for i in range(2)]
        s_wu = [b.sem(f"s_wu{i}") for i in range(2)]
        s_wd = [b.sem(f"s_wd{i}") for i in range(2)]
        s_sw = [b.sem(f"s_sw{i}") for i in range(2)]
        s_si = [b.sem(f"s_si{i}") for i in range(2)]
        s_x2 = [b.sem(f"s_x2{i}") for i in range(2)]
        s_fout = [b.sem(f"s_fout{i}") for i in range(2)]

        for dst, src in ((ident, ident_h), (mdiag, mdiag_h), (biask, biask_h),
                         (ones2h, ones2h_h), (ones64, ones64_h), (wrg, wrg_h)):
            b.op("sync", "dma_start", out=dst[:], in_=src, inc=s_const, inck=16)
        v_const = s_const.n
        v_wrgb = b.op("vector", "tensor_copy", wrgb[:], wrg[:],
                      waits=[(s_const, v_const)], inc=c_ve)

        at = st.enter_context(contextlib.ExitStack())
        asb = lambda name, shape, dt: at.enter_context(nc.sbuf_tensor(name, shape, dt))
        O_sb = asb("O_sb", [128, NP * T], BF16)
        wout_sb = asb("wout_sb", [128, DT * D], BF16)
        qA = [asb(f"qA{i}", [65, T], BF16) for i in range(2)]
        qB = [asb(f"qB{i}", [65, T], BF16) for i in range(2)]
        kA = [asb(f"kA{i}", [65, T], BF16) for i in range(2)]
        kB = [asb(f"kB{i}", [65, T], BF16) for i in range(2)]
        V_sb = [asb(f"V_sb{i}", [128, KT * 130], BF16) for i in range(2)]
        wq_s = [asb(f"wq_s{i}", [128, DT * 128], BF16) for i in range(2)]
        wk_s = [asb(f"wk_s{i}", [128, DT * 128], BF16) for i in range(2)]
        wv_s = [asb(f"wv_s{i}", [128, DT * 128], BF16) for i in range(2)]
        E_b = [asb(f"E_b{i}", [128, T], BF16) for i in range(2)]
        arena = asb("arena", [65, 3 * T], F32)
        mk = asb("mk", [33, 2], F32)
        qsq = asb("qsq", [128, T], BF16)
        bcs = [asb(f"bcs{i}", [64, 512], F32) for i in range(2)]
        s_t = [asb(f"s_t{i}", [128, D], F32) for i in range(2)]
        rsc_lg = asb("rsc_lg", [128, TTN * 2 * E], F32)
        rsc_er = asb("rsc_er", [128, TTN * 2 * E], F32)
        rsc_mx = asb("rsc_mx", [128, 4 * TTN], F32)

        def emit_pair_weights(p, wsem, waits):
            for (dst, c0) in ((wq_s[p % 2], p * 128), (wk_s[p % 2], D + p * 128),
                              (wv_s[p % 2], 2 * D + p * 128)):
                b.op("gpsimd", "dma_start", out=dst.ap().rearrange("p (dt c) -> p dt c", dt=DT),
                     in_=wqkv_h[:, c0:c0 + 128].rearrange("(dt p) c -> p dt c", p=128),
                     waits=waits, inc=wsem, inck=16)
                waits = []
        emit_pair_weights(0, s_wp[0], [])
        emit_pair_weights(1, s_wp[1], [])
        b.op("gpsimd", "dma_start",
             out=wout_sb.ap().rearrange("p (dt c) -> p dt c", dt=DT),
             in_=wout_h.rearrange("(dt p) c -> p dt c", p=128),
             inc=s_wout, inck=16)

        v_xn_evac = [0] * TTN
        with contextlib.ExitStack() as ln:
            lsb = lambda name, shape, dt: ln.enter_context(nc.sbuf_tensor(name, shape, dt))
            x_t = [lsb(f"x_t{i}", [128, D], F32) for i in range(2)]
            xn_tm = [lsb(f"xn_tm{i}", [128, D], BF16) for i in range(2)]
            stats = [lsb(f"stat{i}", [128, 8], F32) for i in range(2)]
            ps_ln = [ln.enter_context(nc.psum_tensor(f"ps_ln{i}", [128, DT * 128], BF16))
                     for i in range(2)]
            x_free = [0, 0]
            stats_free = [0, 0]
            xntm_free = [0, 0]
            psln_free = [0, 0]
            for tt in range(TTN):
                xs, ss = tt % 2, tt % 2
                sx = s_x[xs]
                vx = b.op("sync", "dma_start", out=x_t[xs][:], in_=x_h[tt * 128:(tt + 1) * 128, :],
                          waits=[(c_ve, x_free[xs])], inc=sx, inck=16)
                stt = stats[xs]
                v1 = b.op("scalar", "activation", E_b[0][:, 0:D], x_t[xs][:], AF.Copy,
                          accum_out=stt[:, 0:1],
                          waits=[(sx, vx), (c_ve, stats_free[xs])], inc=c_se)
                v2 = b.op("scalar", "activation", E_b[1][:, 0:D], x_t[xs][:], AF.Square,
                          accum_out=stt[:, 1:2], waits=[(c_se, v1)], inc=c_se)
                v4 = b.op("vector", "tensor_scalar_mul", stt[:, 2:3], stt[:, 0:1], 1.0 / D,
                          waits=[(c_se, v2)], inc=c_ve)
                v5 = b.op("vector", "tensor_scalar_mul", stt[:, 3:4], stt[:, 1:2], 1.0 / D,
                          waits=[(c_ve, v4)], inc=c_ve)
                v6 = b.op("vector", "tensor_mul", stt[:, 4:5], stt[:, 2:3], stt[:, 2:3],
                          waits=[(c_ve, v5)], inc=c_ve)
                v7 = b.op("vector", "tensor_sub", stt[:, 5:6], stt[:, 3:4], stt[:, 4:5],
                          waits=[(c_ve, v6)], inc=c_ve)
                v7b = b.op("vector", "tensor_scalar_add", stt[:, 5:6], stt[:, 5:6], 1e-5,
                           waits=[(c_ve, v7)], inc=c_ve)
                v8 = b.op("scalar", "activation", stt[:, 6:7], stt[:, 5:6], AF.Sqrt,
                          waits=[(c_ve, v7b)], inc=c_se)
                v9 = b.op("vector", "reciprocal", stt[:, 7:8], stt[:, 6:7],
                          waits=[(c_se, v8)], inc=c_ve)
                v10 = b.op("vector", "scalar_tensor_tensor", stt[:, 4:5], stt[:, 2:3], -1.0,
                           stt[:, 7:8], ALU.mult, ALU.mult,
                           waits=[(c_ve, v9)], inc=c_ve)
                v11 = b.op("vector", "tensor_scalar", xn_tm[ss][:], x_t[xs][:],
                           stt[:, 7:8], stt[:, 4:5], ALU.mult, ALU.add,
                           waits=[(c_ve, v10), (c_se, v2), (c_pe, xntm_free[ss])], inc=c_ve)
                x_free[xs] = v11
                stats_free[xs] = v11
                vp = None
                for dt in range(DT):
                    vp = b.op("tensor", "transpose",
                              ps_ln[ss][:, dt * 128:(dt + 1) * 128],
                              xn_tm[ss][:, dt * 128:(dt + 1) * 128], ident[:],
                              waits=([(c_ve, v11), (s_const, v_const),
                                      (c_ve, psln_free[ss])] if dt == 0 else []),
                              inc=(c_pe if dt == DT - 1 else None))
                xntm_free[ss] = vp
                ve = b.op("vector", "tensor_copy",
                          xn.ap().rearrange("p (dt t) -> p dt t", dt=DT)[:, :, tt * 128:(tt + 1) * 128],
                          ps_ln[ss].ap().rearrange("p (dt t) -> p dt t", dt=DT),
                          waits=[(c_pe, vp)], inc=c_ve)
                psln_free[ss] = ve
                v_xn_evac[tt] = ve
        v_xn_all = max(v_xn_evac)
        v_psln_free = [psln_free[0], psln_free[1]]

        pb = [st.enter_context(nc.psum_tensor(f"pb{i}", [128, 512], F32)) for i in range(8)]

        lgall = rsc_lg
        EE = 2 * E
        v_lgc = 0
        bank_free_rt = [0, 0]
        for tt in range(TTN):
            bank = pb[6 + tt % 2]
            vp = None
            for dt in range(DT):
                vp = b.op("tensor", "matmul", bank[:, 0:EE],
                          xn[:, dt * T + tt * 128: dt * T + (tt + 1) * 128],
                          wrgb[:, dt * EE:(dt + 1) * EE],
                          start=(dt == 0), stop=(dt == DT - 1),
                          waits=([(c_ve, max(v_xn_evac[tt], v_wrgb, bank_free_rt[tt % 2]))]
                                 if dt == 0 else []),
                          inc=(c_pe if dt == DT - 1 else None))
            v_lgc = b.op("vector", "tensor_copy", lgall[:, tt * EE:(tt + 1) * EE], bank[:, 0:EE],
                         waits=[(c_pe, vp)], inc=c_ve)
            bank_free_rt[tt % 2] = v_lgc
        lg3 = lgall.ap().rearrange("p (t x) -> p t x", x=EE)
        rv = lg3[:, :, 0:E]
        gv = lg3[:, :, E:EE]
        mxr = rsc_mx[:, 0:TTN]
        mxg = rsc_mx[:, TTN:2 * TTN]
        smr = rsc_mx[:, 2 * TTN:3 * TTN]
        smg = rsc_mx[:, 3 * TTN:4 * TTN]
        era = rsc_er.ap().rearrange("p (t x) -> p t x", x=EE)
        err_ = era[:, :, 0:E]
        erg = era[:, :, E:EE]

        def bc(ap1col, width=E):
            return bass.AP(ap1col.tensor, ap1col.offset,
                           [[ap1col.ap[0][0], 128], [ap1col.ap[1][0], TTN], [0, width]])
        m1 = b.op("vector", "tensor_reduce", mxr.rearrange("p (t e) -> p t e", e=1), rv,
                  AXX, ALU.max, waits=[(c_ve, v_lgc)], inc=c_ve)
        m2 = b.op("vector", "tensor_reduce", mxg.rearrange("p (t e) -> p t e", e=1), gv,
                  AXX, ALU.max, inc=c_ve)
        t1 = b.op("vector", "tensor_tensor", err_, rv, bc(mxr), ALU.subtract,
                  waits=[(c_ve, m2)], inc=c_ve)
        t2 = b.op("vector", "tensor_tensor", erg, gv, bc(mxg), ALU.subtract, inc=c_ve)
        e1 = b.op("scalar", "activation", err_, err_, AF.Exp, waits=[(c_ve, t1)], inc=c_se)
        e2 = b.op("scalar", "activation", erg, erg, AF.Exp, scale=10.0,
                  waits=[(c_ve, t2)], inc=c_se)
        s1r = b.op("vector", "tensor_reduce", smr.rearrange("p (t e) -> p t e", e=1), err_,
                   AXX, ALU.add, waits=[(c_se, e2)], inc=c_ve)
        s2r = b.op("vector", "tensor_reduce", smg.rearrange("p (t e) -> p t e", e=1), erg,
                   AXX, ALU.add, inc=c_ve)
        q1 = b.op("vector", "reciprocal", mxr, smr, waits=[(c_ve, s2r)], inc=c_ve)
        q2 = b.op("vector", "reciprocal", mxg, smg, inc=c_ve)
        n1 = b.op("vector", "tensor_tensor", err_, err_, bc(mxr), ALU.mult,
                  waits=[(c_ve, q2)], inc=c_ve)
        n2 = b.op("vector", "tensor_tensor", erg, erg, bc(mxg), ALU.mult, inc=c_ve)
        cc = b.op("vector", "tensor_tensor", err_, err_, erg, ALU.mult,
                  waits=[(c_ve, n2)], inc=c_ve)
        cs_ = b.op("vector", "tensor_reduce", smr.rearrange("p (t e) -> p t e", e=1), err_,
                   AXX, ALU.add, waits=[(c_ve, cc)], inc=c_ve)
        cs2 = b.op("vector", "tensor_scalar_add", smr, smr, 1e-9, waits=[(c_ve, cs_)], inc=c_ve)
        cr = b.op("vector", "reciprocal", mxr, smr, waits=[(c_ve, cs2)], inc=c_ve)
        cwv = b.op("vector", "tensor_tensor",
                   cw.ap().rearrange("p (t x) -> p t x", x=E), err_, bc(mxr), ALU.mult,
                   waits=[(c_ve, cr)], inc=c_ve)
        v_cw_all = c_ve.n


        vones = 0
        for sl in range(2):
            vones = b.op("vector", "memset", kA[sl][64:65, :], 1.0, inc=c_ve)
            vones = b.op("vector", "memset", kB[sl][64:65, :], 1.0, inc=c_ve)
            for kt in range(KT):
                vones = b.op("vector", "memset", V_sb[sl][:, kt * 130 + 64: kt * 130 + 65], 1.0, inc=c_ve)
                vones = b.op("vector", "memset", V_sb[sl][:, kt * 130 + 129: kt * 130 + 130], 1.0, inc=c_ve)

        slot_pe_free = [0, 0]
        bank_se = {i: 0 for i in range(8)}
        bank_ve = {i: 0 for i in range(8)}
        bank_ve[0] = v_psln_free[0]
        bank_ve[1] = v_psln_free[1]
        bank_ve[6] = bank_free_rt[0]
        bank_ve[7] = bank_free_rt[1]
        v_Ewar = [0, 0]
        sc_rot = 0
        v_rows_last = 0
        v_bcast_pe_c = [0] * QC
        v_recip_prev_c = [0] * QC
        v_oevac_chunk = [0] * QC
        bank_ve_b = [[0], [0]]
        v_qsq_free = [0]
        v_arena_free = [0]
        pending_ochain = [[]]
        st_pair = {}
        emitted_w = {0, 1}

        def flush_ochain():
            while pending_ochain[0]:
                pending_ochain[0].pop(0)()

        def make_qkv_jobs(q):
            sl = q % 2
            wsem = s_wp[sl]
            if q not in emitted_w:
                emit_pair_weights(q, wsem, [(c_pe, slot_pe_free[sl])])
                emitted_w.add(q)
            wtot = wsem.n
            b.op("sync", "dma_start", out=arena[0:1, 2 * T:3 * T],
                 in_=slopeq_h[2 * q:2 * q + 1, :],
                 waits=[(c_ve, v_rows_last)], inc=s_slq[0], inck=16)
            b.op("sync", "dma_start", out=arena[32:33, 2 * T:3 * T],
                 in_=slopeq_h[2 * q + 1:2 * q + 2, :], inc=s_slq[0], inck=16)
            slqtot = s_slq[0].n
            st = dict(v_qe=0, v_ke=0, v_v=0, r6=0, qn_last=0, ksq=0, qsq=0)
            st_pair[q] = st
            jobs = []

            def j_q(c):
                bk = 4 + c % 2
                vp = None
                for dt in range(DT):
                    vp = b.op("tensor", "matmul", pb[bk][:, :],
                              wq_s[sl][:, dt * 128:(dt + 1) * 128],
                              xn[:, dt * T + c * 512: dt * T + (c + 1) * 512],
                              start=(dt == 0), stop=(dt == DT - 1),
                              waits=([(wsem, wtot), (c_pe, slot_pe_free[sl]),
                                      (c_se, bank_se[bk]),
                                      (c_ve, max(bank_ve[bk], v_xn_evac[min(4 * c + 3, TTN - 1)]))]
                                     if dt == 0 else []),
                              inc=(c_pe if dt == DT - 1 else None))
                b.op("scalar", "activation", qA[sl][0:64, c * 512:(c + 1) * 512], pb[bk][0:64, :],
                     AF.Copy, scale=0.125, waits=[(c_pe, vp)], inc=c_se)
                st["v_qe"] = b.op("scalar", "activation", qB[sl][0:64, c * 512:(c + 1) * 512],
                                  pb[bk][64:128, :], AF.Copy, scale=0.125, inc=c_se)
                bank_se[bk] = st["v_qe"]

            def j_k(c):
                bk = 4 + c % 2
                vp = None
                for dt in range(DT):
                    vp = b.op("tensor", "matmul", pb[bk][:, :],
                              wk_s[sl][:, dt * 128:(dt + 1) * 128],
                              xn[:, dt * T + c * 512: dt * T + (c + 1) * 512],
                              start=(dt == 0), stop=(dt == DT - 1),
                              waits=([(c_se, bank_se[bk]),
                                      (c_ve, max(bank_ve[bk], v_xn_evac[min(4 * c + 3, TTN - 1)]))]
                                     if dt == 0 else []),
                              inc=(c_pe if dt == DT - 1 else None))
                b.op("vector", "tensor_copy", kA[sl][0:64, c * 512:(c + 1) * 512], pb[bk][0:64, :],
                     waits=[(c_pe, vp)], inc=c_ve)
                st["v_ke"] = b.op("vector", "tensor_copy", kB[sl][0:64, c * 512:(c + 1) * 512],
                                  pb[bk][64:128, :], inc=c_ve)
                bank_ve[bk] = st["v_ke"]

            def j_v(tt):
                bk = 4 + tt % 2
                vp = None
                for dt in range(DT):
                    vp = b.op("tensor", "matmul", pb[bk][:, 0:128],
                              xn[:, dt * T + tt * 128: dt * T + (tt + 1) * 128],
                              wv_s[sl][:, dt * 128:(dt + 1) * 128],
                              start=(dt == 0), stop=(dt == DT - 1),
                              waits=([(c_se, bank_se[bk]),
                                      (c_ve, max(bank_ve[bk], v_xn_evac[tt]))] if dt == 0 else []),
                              inc=(c_pe if dt == DT - 1 else None))
                b.op("vector", "tensor_copy", V_sb[sl][:, tt * 130: tt * 130 + 64], pb[bk][:, 0:64],
                     waits=[(c_pe, vp)], inc=c_ve)
                st["v_v"] = b.op("vector", "tensor_copy",
                                 V_sb[sl][:, tt * 130 + 65: tt * 130 + 129],
                                 pb[bk][:, 64:128], inc=c_ve)
                bank_ve[bk] = st["v_v"]

            def j_qsq():
                b.op("scalar", "activation", qsq[0:64, :], qA[sl][0:64, :], AF.Square,
                     waits=[(c_se, st["v_qe"]), (c_pe, v_qsq_free[0])], inc=c_se)
                st["qsq"] = b.op("scalar", "activation", qsq[64:128, :], qB[sl][0:64, :],
                                 AF.Square, inc=c_se)

            def j_ksq():
                b.op("scalar", "activation", qsq[0:64, :], kA[sl][0:64, :], AF.Square,
                     waits=[(c_ve, st["v_ke"]), (c_pe, st["qn_last"])], inc=c_se)
                st["ksq"] = b.op("scalar", "activation", qsq[64:128, :], kB[sl][0:64, :],
                                 AF.Square, inc=c_se)

            def j_norm(kind, c):
                bk = 4 + c % 2
                vsq = st["qsq"] if kind == "q" else st["ksq"]
                vp = b.op("tensor", "matmul", pb[bk][0:33, :], ones2h[:, 0:33],
                          qsq[:, c * 512:(c + 1) * 512], start=True, stop=True,
                          waits=[(s_const, v_const), (c_se, max(vsq, bank_se[bk])),
                                 (c_ve, max(bank_ve[bk], v_arena_free[0]))], inc=c_pe)
                if kind == "q":
                    st["qn_last"] = vp
                v_qsq_free[0] = vp
                dst = arena[0:33, c * 512:(c + 1) * 512] if kind == "q" else \
                    arena[0:33, T + c * 512: T + (c + 1) * 512]
                bank_ve[bk] = b.op("vector", "tensor_copy", dst, pb[bk][0:33, :],
                                   waits=[(c_pe, vp)], inc=c_ve)
                st["ne"] = bank_ve[bk]

            def j_rows():
                r1 = b.op("vector", "tensor_reduce", mk[:, 0:1], arena[0:33, T:2 * T], AXX,
                          ALU.max, waits=[(c_ve, st["ne"])], inc=c_ve)
                r2 = b.op("scalar", "activation", mk[:, 1:2], mk[:, 0:1], AF.Sqrt,
                          waits=[(c_ve, r1)], inc=c_se)
                r3 = b.op("scalar", "activation", arena[0:33, T:2 * T], arena[0:33, 0:T], AF.Sqrt,
                          waits=[(c_ve, r1)], inc=c_se)
                r4 = b.op("vector", "tensor_scalar_mul", arena[0:33, 0:T], arena[0:33, T:2 * T],
                          mk[:, 1:2], waits=[(c_se, max(r2, r3))], inc=c_ve)
                r5 = b.op("vector", "tensor_sub", qA[sl][64:65, :], arena[0:1, 2 * T:3 * T],
                          arena[0:1, 0:T],
                          waits=[(c_ve, r4), (s_slq[0], slqtot),
                                 (c_pe, slot_pe_free[sl])], inc=c_ve)
                r6 = b.op("vector", "tensor_sub", qB[sl][64:65, :], arena[32:33, 2 * T:3 * T],
                          arena[32:33, 0:T], inc=c_ve)
                st["r6"] = r6
                v_arena_free[0] = r6
            for c in range(QC):
                jobs.append(lambda c=c: j_q(c))
            for c in range(QC):
                jobs.append(lambda c=c: j_k(c))
            jobs.append(j_qsq)
            nq = list(range(QC))
            nk = list(range(QC))
            vts = list(range(TTN))
            seqj = []
            while vts or nq or nk:
                if vts:
                    seqj.append(("v", vts.pop(0)))
                if nq:
                    seqj.append(("nq", nq.pop(0)))
                elif nk is not None and len(nk) == QC and not nq:
                    seqj.append(("ksq", None))
                    seqj.append(("nk", nk.pop(0)))
                elif nk:
                    seqj.append(("nk", nk.pop(0)))
            for kind, cc in seqj:
                if kind == "v":
                    jobs.append(lambda tt=cc: j_v(tt))
                elif kind == "nq":
                    jobs.append(lambda c=cc: j_norm("q", c))
                elif kind == "ksq":
                    jobs.append(j_ksq)
                else:
                    jobs.append(lambda c=cc: j_norm("k", c))
            def j_rows_wrap():
                j_rows()
            jobs.append(j_rows_wrap)
            return jobs

        def finish_rows(q):
            pass

        jobs_next = make_qkv_jobs(0)
        while jobs_next:
            jobs_next.pop(0)()
        v_rows_last = st_pair[0]["r6"]

        for p in range(NP):
            sl = p % 2
            if p + 1 < NP:
                jobs_next = make_qkv_jobs(p + 1)
            else:
                jobs_next = []
            stp = st_pair[p]
            r6 = stp["r6"]
            for h in range(2):
                qX = qA[sl] if h == 0 else qB[sl]
                kX = kA[sl] if h == 0 else kB[sl]
                hg = 2 * p + h
                orow0 = 64 * h
                vex_kt = [0] * KT
                v_pv_last = 0
                for kt in range(KT + 1):
                    if kt < KT:
                        cs = kt // 4
                        ebk = kt % 2
                        for c in range(0, cs + 1):
                            sbk = 6 + (sc_rot % 2)
                            sc_rot += 1
                            w_mm = [(c_se, bank_se[sbk]), (c_ve, bank_ve[sbk])]
                            if kt == 0 and c == 0:
                                w_mm += [(c_ve, max(stp["v_v"], r6)), (c_se, stp["v_qe"])]
                            vsc = b.op("tensor", "matmul", pb[sbk][:, :],
                                       kX[:, kt * 128:(kt + 1) * 128],
                                       qX[:, c * 512:(c + 1) * 512], start=True, stop=True,
                                       waits=w_mm, inc=c_pe)
                            ew = [(c_pe, vsc), (c_pe, v_Ewar[ebk])]
                            if c == cs:
                                pos = kt % 4
                                w0 = pos * 128
                                vstr = b.op("vector", "tensor_tensor", pb[sbk][:, w0:512],
                                            pb[sbk][:, w0:512],
                                            mdiag[:, MD_OFF[pos] + w0:MD_OFF[pos] + 512],
                                            ALU.add,
                                            waits=[(c_pe, vsc), (s_const, v_const),
                                                   (c_se, bank_se[sbk])], inc=c_ve)
                                bank_ve[sbk] = vstr
                                ew.append((c_ve, vstr))
                            vex = b.op("scalar", "activation",
                                       E_b[ebk][:, c * 512:(c + 1) * 512], pb[sbk][:, :],
                                       AF.Exp, bias=biask[:, hg * KT + kt: hg * KT + kt + 1],
                                       waits=ew + [(s_const, v_const)], inc=c_se)
                            bank_se[sbk] = vex
                        vex_kt[kt] = vex
                    if kt == 1:
                        flush_ochain()
                    if kt >= 1:
                        k2 = kt - 1
                        ebk2 = k2 % 2
                        cs2 = k2 // 4
                        vpv = None
                        for c in range(cs2 + 1):
                            w_pv = [(c_se, vex_kt[k2])] if c == 0 else []
                            if k2 == 4 * c:
                                w_pv.append((c_ve, bank_ve[c]))
                            vpv = b.op("tensor", "matmul", pb[c][0:65, :],
                                       V_sb[sl][:, k2 * 130 + h * 65: k2 * 130 + h * 65 + 65],
                                       E_b[ebk2][:, c * 512:(c + 1) * 512],
                                       start=(k2 == 4 * c), stop=(k2 == KT - 1),
                                       waits=w_pv,
                                       inc=(c_pe if c == cs2 else None))
                        v_Ewar[ebk2] = vpv
                        if k2 == KT - 1:
                            v_pv_last = vpv
                    if kt >= 2:
                        for _ in range(2):
                            if jobs_next:
                                jobs_next.pop(0)()
                v_recip_c = [0] * QC
                for c in range(QC):
                    vsum = b.op("scalar", "activation",
                                arena[64:65, c * 512:(c + 1) * 512], pb[c][64:65, :], AF.Copy,
                                waits=[(c_pe, max(v_pv_last, v_bcast_pe_c[c])),
                                       (c_se, v_recip_prev_c[c])], inc=c_se)
                    vln = b.op("scalar", "activation",
                               arena[64:65, T + c * 512: T + (c + 1) * 512],
                               arena[64:65, c * 512:(c + 1) * 512], AF.Ln,
                               waits=[(c_se, vsum), (c_pe, v_bcast_pe_c[c])], inc=c_se)
                    v_recip_c[c] = b.op("scalar", "activation",
                                        arena[64:65, T + c * 512: T + (c + 1) * 512],
                                        arena[64:65, T + c * 512: T + (c + 1) * 512],
                                        AF.Exp, scale=-1.0,
                                        waits=[(c_se, vln)], inc=c_se)
                    v_recip_prev_c[c] = v_recip_c[c]

                def make_ochain(p=p, orow0=orow0, vrc=tuple(v_recip_c)):
                    def emit():
                        vbc = {}
                        vbe = {}
                        vtt = {}

                        def em_bc(c):
                            bk = 4 + c % 2
                            w_ = [(c_ve, bank_ve[bk]), (c_se, vrc[c]), (s_const, v_const)]
                            if c >= 2:
                                w_.append((c_se, vbe[c - 2]))
                            vbc[c] = b.op("tensor", "matmul", pb[bk][0:64, :],
                                          ones64[64:65, 0:64],
                                          arena[64:65, T + c * 512: T + (c + 1) * 512],
                                          start=True, stop=True, waits=w_, inc=c_pe)
                            v_bcast_pe_c[c] = vbc[c]

                        def em_ev(c):
                            bk = 4 + c % 2
                            w_ = [(c_pe, vbc[c]), (c_se, bank_se[bk])]
                            if c >= 2:
                                w_.append((c_ve, vtt[c - 2]))
                            vbe[c] = b.op("scalar", "activation", bcs[c % 2][:, :],
                                          pb[bk][0:64, :], AF.Copy, waits=w_, inc=c_se)
                            bank_se[bk] = vbe[c]

                        def em_tt(c):
                            vtt[c] = b.op("vector", "tensor_tensor",
                                          O_sb[orow0:orow0 + 64,
                                               p * T + c * 512: p * T + (c + 1) * 512],
                                          pb[c][0:64, :], bcs[c % 2][:, :], ALU.mult,
                                          waits=[(c_se, vbe[c]), (c_ve, bank_ve[c])], inc=c_ve)
                            bank_ve[c] = vtt[c]
                            v_oevac_chunk[c] = vtt[c]
                        final = []
                        ev_done = []
                        tt_done = []
                        bi = 0
                        while len(tt_done) < QC:
                            if bi < QC and (bi < 2 or (bi - 2) < len(ev_done)):
                                final.append(("bc", bi))
                                bi += 1
                                continue
                            ce = len(ev_done)
                            if ce < QC and ce < bi and (ce < 2 or (ce - 2) < len(tt_done)):
                                final.append(("ev", ce))
                                ev_done.append(ce)
                                continue
                            ct = len(tt_done)
                            final.append(("tt", ct))
                            tt_done.append(ct)
                        for kind, c in final:
                            if kind == "bc":
                                em_bc(c)
                            elif kind == "ev":
                                em_ev(c)
                            else:
                                em_tt(c)
                    return emit
                pending_ochain[0] = [make_ochain()]
            while jobs_next:
                jobs_next.pop(0)()
            if p + 1 < NP:
                v_rows_last = st_pair[p + 1]["r6"]
            slot_pe_free[sl] = c_pe.n
        flush_ochain()
        v_oevac = max(v_oevac_chunk)

        sbank_free = {}
        st_free = [0, 0]
        for qt in range(TTN):
            slo = qt % 2
            vlast = 0
            for c in range(HCN):
                bank = pb[c + 2 * (qt % 2)]
                vp = None
                for pp in range(DT):
                    vp = b.op("tensor", "matmul", bank[:, 0:HCW],
                              O_sb[:, pp * T + qt * 128: pp * T + (qt + 1) * 128],
                              wout_sb[:, pp * D + c * HCW: pp * D + (c + 1) * HCW],
                              start=(pp == 0), stop=(pp == DT - 1),
                              waits=([(c_ve, max(v_oevac, sbank_free.get(id(bank), 0))),
                                      (s_wout, 16)] if pp == 0 else []),
                              inc=(c_pe if pp == DT - 1 else None))
                vlast = b.op("vector", "tensor_copy", s_t[slo][:, c * HCW:(c + 1) * HCW],
                             bank[:, 0:HCW],
                             waits=[(c_pe, vp), (s_sw[slo], st_free[slo])], inc=c_ve)
                sbank_free[id(bank)] = vlast
            st_free[slo] = b.op("sync", "dma_start", out=sscr_h[qt * 128:(qt + 1) * 128, :],
                                in_=s_t[slo][:], waits=[(c_ve, vlast)], inc=s_sw[slo], inck=16)

        at.close()
        v_sproj_ve = c_ve.n
        b.barrier()

        moe_st = contextlib.ExitStack()
        with moe_st:
            moe = moe_st.enter_context(nc.sbuf_tensor("moe", [128, TTN * D], F32))
            x2 = [moe_st.enter_context(nc.sbuf_tensor(f"x2_{i}", [128, D], F32)) for i in range(2)]
            s_in = [moe_st.enter_context(nc.sbuf_tensor(f"s_in{i}", [128, D], F32)) for i in range(2)]
            moe_last = [[0, 0] for _ in range(TTN)]
            first_acc_tt = [True] * TTN
            with contextlib.ExitStack() as f1:
                gsb = lambda name, shape, dt: f1.enter_context(nc.sbuf_tensor(name, shape, dt))
                h_sb = gsb("h_sb", [128, NFQ * T], BF16)
                wgs = [gsb(f"wgs{i}", [128, DT * FS], BF16) for i in range(2)]
                wus = [gsb(f"wus{i}", [128, DT * FS], BF16) for i in range(2)]
                wds = [gsb(f"wds{i}", [128, NFQ * D], BF16) for i in range(2)]
                sg = gsb("sg", [128, 2 * T], BF16)

                wgs_free = [0, 0]
                wus_free = [0, 0]
                wds_free = [0, 0]
                v_sg_war = [0, 0]
                v_silu_last = 0
                v_htt_last = v_sproj_ve
                fbank_ve = [v_sproj_ve] * 8
                v_h_ft = [0] * NFQ
                v_down_done = 0
                nslice = 0
                for e in range(E):
                    for qt in range(4):
                        for fe in range(SPQ):
                            wsl = nslice % 2
                            nslice += 1
                            f0 = qt * FQ + fe * FS
                            b.op("gpsimd", "dma_start",
                                 out=wgs[wsl].ap().rearrange("p (dt f) -> p dt f", dt=DT),
                                 in_=wg_h[e, :, f0:f0 + FS].rearrange("(dt p) f -> p dt f", p=128),
                                 waits=[(c_pe, wgs_free[wsl])], inc=s_wg[wsl], inck=16)
                            b.op("gpsimd", "dma_start",
                                 out=wus[wsl].ap().rearrange("p (dt f) -> p dt f", dt=DT),
                                 in_=wu_h[e, :, f0:f0 + FS].rearrange("(dt p) f -> p dt f", p=128),
                                 waits=[(c_pe, wus_free[wsl])], inc=s_wu[wsl], inck=16)
                            for ftl in range(NFS):
                                ftq = fe * NFS + ftl
                                vg = None
                                for dt in range(DT):
                                    for c in range(QC):
                                        vg = b.op("tensor", "matmul", pb[c][:, :],
                                                  wgs[wsl][:, dt * FS + ftl * 128: dt * FS + (ftl + 1) * 128],
                                                  xn[:, dt * T + c * 512: dt * T + (c + 1) * 512],
                                                  start=(dt == 0), stop=(dt == DT - 1),
                                                  waits=([(s_wg[wsl], s_wg[wsl].n),
                                                          (c_se, v_silu_last),
                                                          (c_ve, max(max(fbank_ve[0:4]), v_xn_all))]
                                                         if (dt == 0 and c == 0) else []),
                                                  inc=(c_pe if (dt == DT - 1 and c == QC - 1) else None))
                                if ftl == NFS - 1:
                                    wgs_free[wsl] = vg
                                sgh = (ftq % 2) * T
                                vsl = 0
                                for c in range(QC):
                                    vsl = b.op("scalar", "activation",
                                               sg[:, sgh + c * 512: sgh + (c + 1) * 512],
                                               pb[c][:, :], AF.Silu,
                                               waits=([(c_pe, vg), (c_ve, v_sg_war[ftq % 2])]
                                                      if c == 0 else []), inc=c_se)
                                v_silu_last = vsl
                                vu = None
                                for dt in range(DT):
                                    for c in range(QC):
                                        vu = b.op("tensor", "matmul", pb[4 + c][:, :],
                                                  wus[wsl][:, dt * FS + ftl * 128: dt * FS + (ftl + 1) * 128],
                                                  xn[:, dt * T + c * 512: dt * T + (c + 1) * 512],
                                                  start=(dt == 0), stop=(dt == DT - 1),
                                                  waits=([(s_wu[wsl], s_wu[wsl].n),
                                                          (c_ve, max(v_htt_last, max(fbank_ve[4:8])))]
                                                         if (dt == 0 and c == 0) else []),
                                                  inc=(c_pe if (dt == DT - 1 and c == QC - 1) else None))
                                if ftl == NFS - 1:
                                    wus_free[wsl] = vu
                                vh = 0
                                for c in range(QC):
                                    vh = b.op("vector", "tensor_tensor",
                                              h_sb[:, ftq * T + c * 512: ftq * T + (c + 1) * 512],
                                              pb[4 + c][:, :], sg[:, sgh + c * 512: sgh + (c + 1) * 512],
                                              ALU.mult,
                                              waits=([(c_pe, max(vu, v_down_done)), (c_se, vsl)]
                                                     if c == 0 else []), inc=c_ve)
                                v_htt_last = vh
                                v_sg_war[ftq % 2] = vh
                                v_h_ft[ftq] = vh
                        wdsl = (e * 4 + qt) % 2
                        b.op("gpsimd", "dma_start",
                             out=wds[wdsl].ap().rearrange("p (ft d) -> p ft d", ft=NFQ),
                             in_=wd_h[e, qt * FQ:(qt + 1) * FQ, :].rearrange("(ft p) d -> p ft d", p=128),
                             waits=[(c_pe, wds_free[wdsl])], inc=s_wd[wdsl], inck=16)
                        vdn = None
                        for tg in range(TTN // 2):
                            bset = (tg % 2) * 4
                            for ftq in range(NFQ):
                                for ttl in range(2):
                                    tt = tg * 2 + ttl
                                    for hc in range(HCN):
                                        bank = pb[bset + ttl * 2 + hc]
                                        w_ = []
                                        if ttl == 0 and hc == 0 and tg <= 1:
                                            w_ = [(c_ve, v_h_ft[ftq])]
                                        if ftq == 0 and ttl == 0 and hc == 0:
                                            w_ += [(s_wd[wdsl], s_wd[wdsl].n),
                                                   (c_se, v_silu_last),
                                                   (c_ve, max(fbank_ve[bset:bset + 4]))]
                                        vdn = b.op("tensor", "matmul", bank[:, 0:HCW],
                                                   h_sb[:, ftq * T + tt * 128: ftq * T + (tt + 1) * 128],
                                                   wds[wdsl][:, ftq * D + hc * HCW: ftq * D + (hc + 1) * HCW],
                                                   start=(ftq == 0), stop=(ftq == NFQ - 1),
                                                   waits=w_,
                                                   inc=(c_pe if (ftq == NFQ - 1 and ttl == 1 and hc == HCN - 1) else None))
                            for ttl in range(2):
                                tt = tg * 2 + ttl
                                for hc in range(HCN):
                                    bank = pb[bset + ttl * 2 + hc]
                                    dst = moe[:, tt * D + hc * HCW: tt * D + (hc + 1) * HCW]
                                    cwc = cw[:, tt * E + e: tt * E + e + 1]
                                    if first_acc_tt[tt]:
                                        vm = b.op("vector", "tensor_scalar_mul", dst, bank[:, 0:HCW], cwc,
                                                  waits=[(c_pe, vdn), (c_ve, v_cw_all)], inc=c_ve)
                                    else:
                                        vm = b.op("vector", "scalar_tensor_tensor", dst, bank[:, 0:HCW],
                                                  cwc, dst, ALU.mult, ALU.add,
                                                  waits=[(c_pe, vdn), (c_ve, moe_last[tt][hc])], inc=c_ve)
                                    moe_last[tt][hc] = vm
                                    fbank_ve[bset + ttl * 2 + hc] = vm
                                first_acc_tt[tt] = False
                        v_down_done = c_pe.n

            x2_free = [0, 0]
            si_free = [0, 0]
            o_free = [0, 0]
            for tt in range(TTN):
                slo = tt % 2
                vx = b.op("gpsimd", "dma_start", out=x2[slo][:], in_=x_h[tt * 128:(tt + 1) * 128, :],
                          waits=[(c_ve, x2_free[slo])], inc=s_x2[slo], inck=16)
                vsi = b.op("gpsimd", "dma_start", out=s_in[slo][:],
                           in_=sscr_h[tt * 128:(tt + 1) * 128, :],
                           waits=[(c_ve, si_free[slo]), (s_fout[slo], o_free[slo]),
                                  (s_sw[0], s_sw[0].n), (s_sw[1], s_sw[1].n)],
                           inc=s_si[slo], inck=16)
                mslice = moe[:, tt * D:(tt + 1) * D]
                vg = b.op("scalar", "activation", mslice, mslice, AF.Silu,
                          waits=[(c_ve, max(moe_last[tt][0], moe_last[tt][1]))], inc=c_se)
                vc = b.op("vector", "tensor_mul", mslice, s_in[slo][:], mslice,
                          waits=[(c_se, vg), (s_si[slo], vsi)], inc=c_ve)
                vo = b.op("vector", "tensor_add", s_in[slo][:], mslice, x2[slo][:],
                          waits=[(c_ve, vc), (s_x2[slo], vx)], inc=c_ve)
                x2_free[slo] = vo
                si_free[slo] = vo
                o_free[slo] = b.op("sync", "dma_start", out=out_h[tt * 128:(tt + 1) * 128, :],
                                   in_=s_in[slo][:], waits=[(c_ve, vo)], inc=s_fout[slo], inck=16)
            b.raw("sync", lambda e, s0=s_fout[0], v0=s_fout[0].n, s1=s_fout[1], v1=s_fout[1].n:
                  (e.wait_ge(s0.h, v0), e.wait_ge(s1.h, v1)))

        blk = st.enter_context(nc.Block())
        for eng in ENGINES:
            fns = b.q[eng]
            def runner(e, fns=fns):
                for f in fns:
                    f(e)
            getattr(blk, eng)(runner)
    return nc


def prep_inputs(L0_attn_x, L0_attn_W_qkv, L0_attn_W_out, L0_ffn_w_router, L0_ffn_w_gating,
                L0_ffn_w_gate, L0_ffn_w_up, L0_ffn_w_down, T, D, NH):
    DT = D // 128
    E = np.asarray(L0_ffn_w_router).shape[1]
    x = np.ascontiguousarray(np.asarray(L0_attn_x, dtype=np.float32))
    wqkv = np.ascontiguousarray(np.asarray(L0_attn_W_qkv, dtype=np.float32))
    wout = np.ascontiguousarray(np.asarray(L0_attn_W_out, dtype=np.float32))
    wr = np.asarray(L0_ffn_w_router, dtype=np.float32).reshape(DT, 128, E)
    wgt = np.asarray(L0_ffn_w_gating, dtype=np.float32).reshape(DT, 128, E)
    wrg = np.ascontiguousarray(
        np.concatenate([wr, wgt], axis=2).transpose(1, 0, 2).reshape(128, DT * 2 * E))
    wg = np.ascontiguousarray(np.asarray(L0_ffn_w_gate, dtype=np.float32))
    wu = np.ascontiguousarray(np.asarray(L0_ffn_w_up, dtype=np.float32))
    wd = np.ascontiguousarray(np.asarray(L0_ffn_w_down, dtype=np.float32))
    cst = make_consts(T, NH)
    base = dict(wqkv=wqkv, wout=wout, wrg=wrg, wg=wg, wu=wu, wd=wd,
                mdiag=cst["mdiag"], biask=cst["biask"], slopeq=cst["slopeq"],
                ones2h=cst["ones2h"], ones64=cst["ones64"], ident=cst["ident"])
    return x, base


def kernel(L0_attn_x, L0_attn_W_qkv, L0_attn_W_out, L0_ffn_w_router, L0_ffn_w_gating,
           L0_ffn_w_gate, L0_ffn_w_up, L0_ffn_w_down):
    T, D, NH, E, F = 2048, 1024, 16, 8, 4096
    x, base = prep_inputs(L0_attn_x, L0_attn_W_qkv, L0_attn_W_out, L0_ffn_w_router,
                          L0_ffn_w_gating, L0_ffn_w_gate, L0_ffn_w_up, L0_ffn_w_down, T, D, NH)
    nc = build(T, D, NH, E, F)
    in_maps = [dict(base, x=np.ascontiguousarray(x[bb])) for bb in range(x.shape[0])]
    res = run_bass_kernel_spmd(nc, in_maps, core_ids=list(range(8)))
    out = np.stack([np.asarray(r["out"]) for r in res.results], axis=0).astype(np.float32)
    return out
